# version 01
# baseline (speedup 1.0000x reference)
import numpy as np
import ml_dtypes
from contextlib import ExitStack

import concourse.bass as bass
import concourse.mybir as mybir
from concourse.bass_utils import run_bass_kernel_spmd

F32 = mybir.dt.float32
BF16 = mybir.dt.bfloat16
I32 = mybir.dt.int32
AF = mybir.ActivationFunctionType
ALU = mybir.AluOpType
AXX = mybir.AxisListType.X

T = 2048
NT = 16
D = 1024
S = 8192
SCALE0 = 0.125
DFF0 = 4096
DFF1 = 3072
EPS_LN = 1e-5
EPS_LA = 1e-6
MAGIC = 12582912.0
TWO_PI = float(2.0 * np.pi)
RGROUPS = [[0, 1, 2, 3], [4, 5, 6, 7]]
CONSTVALS = [EPS_LN, EPS_LA, 1.0 / D, -0.5, 0.1, 1.0 / 127.0, -1.0, 0.5,
             float(np.pi / 2), 0.125, 2.0, -2.0, 1.0, 0.0, 3.0, 4.0]

ENGINES = ("tensor", "scalar", "vector", "sync", "gpsimd")


def rank_of_chunk(c):
    return c if c < 4 else 7 - c


def half_of_chunk(c):
    return 0 if c < 4 else 1


class Prog:
    def __init__(self, nc, sem_names):
        self.nc = nc
        self.ctx = ExitStack()
        self.sems = {n: self.ctx.enter_context(nc.semaphore(n)) for n in sem_names}
        self.cnt = {n: 0 for n in sem_names}
        self.ops = {e: [] for e in ENGINES}
        self.waited = {e: {n: 0 for n in sem_names} for e in ENGINES}

    def wait(self, eng, tok):
        if tok is None:
            return
        if isinstance(tok, list):
            best = {}
            for t in tok:
                if t is None:
                    continue
                tl = t if isinstance(t, list) else [t]
                for sem, val in tl:
                    best[sem] = max(best.get(sem, 0), val)
            for sem, val in best.items():
                self.wait(eng, (sem, val))
            return
        sem, val = tok
        if val <= 0 or val <= self.waited[eng][sem]:
            return
        self.waited[eng][sem] = val
        s = self.sems[sem]
        self.ops[eng].append(lambda E, s=s, v=val: E.wait_ge(s, v))

    def run(self, eng, fn, inc=None, inc_by=1):
        tok = None
        if inc is not None:
            self.cnt[inc] += inc_by
            tok = (inc, self.cnt[inc])
            s = self.sems[inc]

            def wrapped(E, fn=fn, s=s, n=inc_by):
                ins = fn(E)
                assert ins is not None, "inc-op closure must return its last instruction"
                ins.then_inc(s, n)
            self.ops[eng].append(wrapped)
        else:
            self.ops[eng].append(fn)
        return tok

    def barrier(self):
        self.cnt["ph"] += len(ENGINES)
        target = self.cnt["ph"]
        s = self.sems["ph"]
        for e in ENGINES:
            self.ops[e].append(lambda E, s=s: E.sem_inc(s, 1))
            self.waited[e]["ph"] = target
            self.ops[e].append(lambda E, s=s, v=target: E.wait_ge(s, v))

    def emit_block(self):
        block = self.ctx.enter_context(self.nc.Block())
        for e in ENGINES:
            ops = self.ops[e]

            def body(E, ops=ops):
                for op in ops:
                    op(E)
            getattr(block, e)(body)


class Builder:
    def __init__(self, n_blocks=4, debug_taps=False):
        self.n_blocks = n_blocks
        nc = self.nc = bass.Bass("TRN2", target_bir_lowering=False,
                                 detect_race_conditions=False)
        iok = "ExternalOutput" if debug_taps else "Internal"
        dd = {}

        def din(name, shape, dtype=F32):
            dd[name] = nc.dram_tensor(name, shape, dtype, kind="ExternalInput")
            return dd[name]

        self.d = dd
        din("xown", [T, D])
        din("xfull", [S, D])
        din("Wq_s", [D, 256]); din("bq_s", [256])
        din("Wk_s", [D, 256]); din("bk_s", [256])
        din("Wv_s", [D, 256])
        din("Wm", [D, D]); din("bm", [D]); din("bv", [D])
        din("W1", [D, DFF0]); din("b1", [DFF0])
        din("qph", [DFF0]); din("qwr", [DFF0]); din("qwi", [DFF0])
        din("W2", [DFF0, D]); din("b2", [D])
        din("Wq1", [256, D]); din("Wk1", [256, D])
        din("Wv1", [256, D]); din("Wo1", [D, 256])
        din("wg", [DFF1, D], I32); din("wu", [DFF1, D], I32)
        din("wd", [D, DFF1], I32)
        din("ident", [128, 128], BF16)
        din("trimask", [128, 128], BF16)
        din("sel", [128, 16])
        din("sel64", [64, 128])
        din("constv", [128, 16])
        din("amrows", [1, 8], I32)

        self.y_out = nc.dram_tensor("yout", [T, D], F32, kind="ExternalOutput")
        self.Q_loc = nc.dram_tensor("Q_loc", [2, 128, S], BF16, kind=iok)
        self.K_loc = nc.dram_tensor("K_loc", [2, 128, S], BF16, kind=iok)
        self.V_loc = nc.dram_tensor("V_loc", [4, S, 64], BF16, kind=iok)
        self.a2a_in = nc.dram_tensor("a2a_in", [1024, 2048], BF16, kind="Internal")
        self.a2a_out = nc.dram_tensor("a2a_out", [4096, 2048], BF16, kind="Internal")
        self.ls_in = nc.dram_tensor("ls_in", [64, 520], F32, kind="Internal")
        self.ls_out = nc.dram_tensor("ls_out", [256, 520], F32, kind="Internal")
        self.wg_bf = nc.dram_tensor("wg_bf", [8, DFF1, 128], BF16, kind="Internal")
        self.wu_bf = nc.dram_tensor("wu_bf", [8, DFF1, 128], BF16, kind="Internal")
        self.wd_bf = nc.dram_tensor("wd_bf", [24, D, 128], BF16, kind="Internal")
        self.wgT_d = nc.dram_tensor("wgT_d", [D, DFF1], BF16, kind="Internal")
        self.wuT_d = nc.dram_tensor("wuT_d", [D, DFF1], BF16, kind="Internal")
        self.wdT_d = nc.dram_tensor("wdT_d", [DFF1, D], BF16, kind="Internal")
        self.debug_taps = debug_taps
        if debug_taps:
            self.tap_a2a = nc.dram_tensor("tap_a2a", [1024, 2048], BF16,
                                          kind="ExternalOutput")
            self.tap_a2ao = nc.dram_tensor("tap_a2ao", [4096, 2048], BF16,
                                           kind="ExternalOutput")
            self.tap_amT = nc.dram_tensor("tap_amT", [128, 16384], BF16,
                                          kind="ExternalOutput")
            self.tap_wgT = nc.dram_tensor("tap_wgT", [D, DFF1], BF16,
                                          kind="ExternalOutput")
            self.tap_wdT = nc.dram_tensor("tap_wdT", [DFF1, D], BF16,
                                          kind="ExternalOutput")
            self.tap_h = nc.dram_tensor("tap_h", [128, 24 * 1024], BF16,
                                        kind="ExternalOutput")

        dma_slots = ["dS", "dR", "dW", "dX0", "dX1", "dSQ", "dSK", "dSV",
                     "dQ0", "dQ1", "dK0", "dK1", "dV0", "dV1", "dSA", "dSB", "dM",
                     "dO", "dP", "dC0", "dC1", "dC2", "dC3", "dE0", "dE1",
                     "dE2", "dE3", "dD0", "dD1", "dL", "dG", "dWC"]
        self.p = Prog(nc, ["pe", "ac", "ve", "ag", "ph"] + dma_slots)

        A = nc.alloc_sbuf_tensor
        self.RS = A("RS", [128, NT * D], F32)
        self.id_sb = A("id_sb", [128, 128], BF16)
        self.tri_sb = A("tri_sb", [128, 128], BF16)
        self.ones128 = A("ones128", [1, 128], F32)
        self.sel64_sb = A("sel64_sb", [64, 128], F32)
        self.cb_tile = A("cb_tile", [128, D], F32)
        self.b2_tile = A("b2_tile", [128, D], F32)
        self.colq = A("colq", [128, 2], F32)
        self.colk = A("colk", [128, 2], F32)
        self.colbv = A("colbv", [128, 8], BF16)
        self.col1 = A("col1", [128, 32], F32)
        self.thc = A("thc", [128, 32], F32)
        self.wrc = A("wrc", [128, 32], F32)
        self.wic = A("wic", [128, 32], F32)
        self.Cc = A("Cc", [128, 32], F32)
        self.selc = A("selc", [128, 16], F32)
        self.junk_bf = A("junk_bf", [128, D], BF16)
        self.cols = A("cols", [128, 16], F32)
        self.constv_sb = A("constv_sb", [128, 16], F32)
        for i, v in enumerate(CONSTVALS):
            nc.const_aps.register(F32, v, self.constv_sb[:, i:i + 1])

        self.AR_BF_COLS = 47616
        self.AR_F_COLS = 8704
        self.ar_bf = A("ar_bf", [128, self.AR_BF_COLS], BF16)
        self.ar_f = A("ar_f", [128, self.AR_F_COLS], F32)
        self.bf_off = 0
        self.f_off = 0

    def ar_reset(self):
        self.bf_off = 0
        self.f_off = 0

    def bf(self, cols):
        a = self.ar_bf[:, self.bf_off:self.bf_off + cols]
        self.bf_off += cols
        assert self.bf_off <= self.AR_BF_COLS, f"bf arena overflow {self.bf_off}"
        return a

    def f32(self, cols):
        a = self.ar_f[:, self.f_off:self.f_off + cols]
        self.f_off += cols
        assert self.f_off <= self.AR_F_COLS, f"f32 arena overflow {self.f_off}"
        return a

    def rs_tile(self, t):
        return self.RS[:, D * t:D * (t + 1)]

    def setup(self):
        p, d = self.p, self.d
        src = d["xown"].ap().rearrange("(t p) d -> p t d", p=128)
        dst = self.RS[:].rearrange("p (t d) -> p t d", d=D)
        self.tok_rs_in = p.run("sync", lambda E: E.dma_start(out=dst, in_=src),
                               inc="dR", inc_by=16)
        g = "gpsimd"
        toks = []
        toks.append(p.run(g, lambda E: E.dma_start(out=self.id_sb[:], in_=d["ident"].ap()), inc="dS", inc_by=16))
        toks.append(p.run(g, lambda E: E.dma_start(out=self.tri_sb[:], in_=d["trimask"].ap()), inc="dS", inc_by=16))
        toks.append(p.run(g, lambda E: E.dma_start(out=self.selc[:], in_=d["sel"].ap()), inc="dS", inc_by=16))
        toks.append(p.run(g, lambda E: E.dma_start(out=self.sel64_sb[:], in_=d["sel64"].ap()), inc="dS", inc_by=16))
        toks.append(p.run(g, lambda E: E.dma_start(out=self.constv_sb[:], in_=d["constv"].ap()), inc="dS", inc_by=16))

        def colload(dstap, srcname, pp=128):
            srcap = d[srcname].ap().rearrange("(c p) -> p c", p=pp)
            return p.run(g, lambda E: E.dma_start(out=dstap, in_=srcap,
                                                  allow_slow_non_contiguous=True),
                         inc="dS", inc_by=16)
        colload(self.colq[:], "bq_s")
        colload(self.colk[:], "bk_s")
        colload(self.colbv[:], "bv")

        self.tok_setup = ("dS", p.cnt["dS"])
        self.tok_consts = self.tok_setup
        self.tok_colq = self.tok_setup
        self.tok_colk = self.tok_setup
        self.tok_colbv = self.tok_setup
        p.wait("vector", self.tok_setup)
        self.tok_colq8 = p.run("vector", lambda E: E.tensor_scalar_mul(
            self.colq[:], self.colq[:], 0.125), inc="ve")
        p.run("vector", lambda E: E.memset(self.ones128[:], 1.0))
        p.run("vector", lambda E: E.memset(self.drow[:], 0.0))

    def load_w_bf(self, name, kchunks, cols, tag):
        p, d = self.p, self.d
        buf = self.bf(kchunks * cols)
        src = d[name].ap().rearrange("(k p) c -> p k c", p=128)
        dst = buf.rearrange("p (k c) -> p k c", c=cols)
        tok = p.run("gpsimd", lambda E: E.dma_start(out=dst, in_=src),
                    inc="dW", inc_by=16)
        return buf, tok

    def phase_b1(self):
        p, d = self.p, self.d
        self.ar_reset()
        for e in ("tensor", "vector", "scalar"):
            p.wait(e, self.tok_setup)
        Wq_bf, tok_wq = self.load_w_bf("Wq_s", 8, 256, "wq")
        Wk_bf, tok_wk = self.load_w_bf("Wk_s", 8, 256, "wk")
        Wv_bf, tok_wv = self.load_w_bf("Wv_s", 8, 256, "wv")
        p.wait("vector", ("dW", 48))
        tok_wq8 = p.run("vector", lambda E_: E_.tensor_scalar_mul(
            Wq_bf, Wq_bf, 0.125), inc="ve")

        xb2 = [self.bf(4096), self.bf(4096)]
        xT = self.bf(4096)
        QT_win = self.bf(1024)
        KT_win = self.bf(1024)
        VT_win = self.bf(1024)

        with ExitStack() as ps_ctx:
            E = ps_ctx.enter_context
            tp_ps = [E(self.nc.psum_tensor(f"tpB1_{i}", [128, 512], BF16)) for i in range(2)]
            qk_ps = [E(self.nc.psum_tensor(f"qkB1_{i}", [128, 512], F32)) for i in range(2)]
            v_ps = [E(self.nc.psum_tensor(f"vB1_{i}", [128, 256], F32)) for i in range(2)]

            xw_tok = [None, None]
            xw_free = [None, None]
            tp_free = [None, None]
            qk_free = [None, None]
            v_free = [None, None]
            stage_done = [None, None, None]
            xT_gate = None
            self.tok_b1_stage = []

            for w in range(16):
                b = w % 2
                p.wait("gpsimd", xw_free[b])
                src = d["xfull"].ap()[512 * w:512 * (w + 1), :].rearrange("(s p) d -> p s d", p=128)
                dstw = xb2[b].rearrange("p (s d) -> p s d", d=D)
                xw_tok[b] = p.run("gpsimd", lambda E_, o=dstw, i=src: E_.dma_start(out=o, in_=i),
                                  inc="dX0" if b == 0 else "dX1", inc_by=16)
                ev_last = None
                for gk in range(8):
                    psel = gk % 2
                    p.wait("tensor", xw_tok[b])
                    p.wait("tensor", tp_free[psel])
                    ttok = None
                    for s_i in range(4):
                        blk = xb2[b][:, 1024 * s_i + 128 * gk:1024 * s_i + 128 * (gk + 1)]
                        ttok = p.run("tensor",
                                     lambda E_, o=tp_ps[psel][:, 128 * s_i:128 * (s_i + 1)], i=blk:
                                     E_.transpose(o, i, self.id_sb[:]),
                                     inc="pe" if s_i == 3 else None)
                    p.wait("vector", ttok)
                    p.wait("vector", xT_gate)
                    etok = p.run("vector", lambda E_, o=xT[:, 512 * gk:512 * (gk + 1)],
                                 i=tp_ps[psel][:]: E_.tensor_copy(o, i), inc="ve")
                    tp_free[psel] = etok
                    ev_last = etok
                xw_free[b] = ev_last
                p.wait("tensor", ev_last)
                p.wait("tensor", ("dW", 48))
                p.wait("tensor", tok_wq8)
                p.wait("scalar", [self.tok_colq, self.tok_colk])
                qk_evacs = []
                for (Wbf, colb, OUT, scl) in ((Wq_bf, self.colq, QT_win, 0.125),
                                              (Wk_bf, self.colk, KT_win, 1.0)):
                    for hpb in range(2):
                        psel = len(qk_evacs) % 2
                        p.wait("tensor", qk_free[psel])
                        mtok = None
                        for k in range(8):
                            mtok = p.run("tensor", lambda E_, o=qk_ps[psel][:],
                                         l=Wbf[:, 256 * k + 128 * hpb:256 * k + 128 * (hpb + 1)],
                                         r=xT[:, 512 * k:512 * (k + 1)],
                                         st=(k == 0), sp=(k == 7):
                                         E_.matmul(o, l, r, start=st, stop=sp),
                                         inc="pe" if k == 7 else None)
                        p.wait("scalar", mtok)
                        p.wait("scalar", self.tok_colq8)
                        etok = p.run("scalar", lambda E_, o=OUT[:, 512 * hpb:512 * (hpb + 1)],
                                     i=qk_ps[psel][:], bcol=colb[:, hpb:hpb + 1]:
                                     E_.activation(o, i, AF.Identity, bias=bcol),
                                     inc="ac")
                        qk_free[psel] = etok
                        qk_evacs.append(etok)
                v_evacs = []
                for ts in range(4):
                    psel = ts % 2
                    p.wait("tensor", v_free[psel])
                    mtok = None
                    for k in range(8):
                        mtok = p.run("tensor", lambda E_, o=v_ps[psel][:],
                                     l=xT[:, 512 * k + 128 * ts:512 * k + 128 * (ts + 1)],
                                     r=Wv_bf[:, 256 * k:256 * (k + 1)],
                                     st=(k == 0), sp=(k == 7):
                                     E_.matmul(o, l, r, start=st, stop=sp),
                                     inc="pe" if k == 7 else None)
                    p.wait("vector", mtok)
                    etok = p.run("vector", lambda E_, o=VT_win[:, 256 * ts:256 * (ts + 1)],
                                 i=v_ps[psel][:]: E_.tensor_copy(o, i), inc="ve")
                    v_free[psel] = etok
                    v_evacs.append(etok)
                xT_gate = mtok
                p.wait("sync", qk_evacs)
                sts = []
                for hpb in range(2):
                    sts.append(p.run("sync", lambda E_, o=self.Q_loc.ap()[hpb][:, 512 * w:512 * (w + 1)],
                                     i=QT_win[:, 512 * hpb:512 * (hpb + 1)]:
                                     E_.dma_start(out=o, in_=i), inc="dSQ", inc_by=16))
                    sts.append(p.run("sync", lambda E_, o=self.K_loc.ap()[hpb][:, 512 * w:512 * (w + 1)],
                                     i=KT_win[:, 512 * hpb:512 * (hpb + 1)]:
                                     E_.dma_start(out=o, in_=i), inc="dSK", inc_by=16))
                p.wait("sync", v_evacs)
                vsrc3 = VT_win.rearrange("p (s c) -> p s c", c=256)
                for h in range(4):
                    vdst = self.V_loc.ap()[h][512 * w:512 * (w + 1), :].rearrange("(s p) c -> p s c", p=128)
                    sts.append(p.run("sync", lambda E_, o=vdst, i=vsrc3[:, :, 64 * h:64 * (h + 1)]:
                                     E_.dma_start(out=o, in_=i), inc="dSV", inc_by=16))
                self.tok_b1_stage = [("dSQ", p.cnt["dSQ"]), ("dSK", p.cnt["dSK"]),
                                     ("dSV", p.cnt["dSV"])]
                for e in ("scalar", "vector"):
                    p.wait(e, self.tok_b1_stage)

    def phase_b3(self):
        p, d, nc = self.p, self.d, self.nc
        self.ar_reset()
        qbuf = [self.bf(512), self.bf(512)]
        kbuf = [self.bf(1024), self.bf(1024)]
        vbuf = [self.bf(1040), self.bf(1040)]
        pt_sb = [self.bf(1536), self.bf(1536)]
        attnN = self.bf(1024)
        self.Wm_bf = self.bf(8 * D)
        self.drow = self.f32(512)[0:64, :]
        self.amT_b4 = self.bf(16384)
        self.bmrow = self.f32(1024)[0:1, :]
        self.rrow_b4 = self.f32(1024)[0:1, :]
        self.scr = [(self.f32(1024), self.f32(1024)), (self.f32(1024), self.f32(1024))]
        p.run("vector", lambda E_: E_.memset(self.drow[:, :], 0.0))

        with ExitStack() as ps_ctx:
            E = ps_ctx.enter_context
            st_ps = E(nc.psum_tensor("st_ps", [128, 3072], F32))
            av_ps = E(nc.psum_tensor("av_ps", [65, 1024], F32))

            vone_toks = []
            for vb in vbuf:
                ap = vb.rearrange("p (a c) -> p a c", c=65)[:, :, 64:65]
                vone_toks.append(p.run("vector", lambda E_, a=ap: E_.memset(a, 1.0), inc="ve"))
            src = d["Wm"].ap().rearrange("(k p) m -> p k m", p=128)
            dstw = self.Wm_bf.rearrange("p (k m) -> p k m", m=D)
            self.tok_wm = p.run("gpsimd", lambda E_: E_.dma_start(out=dstw, in_=src),
                                inc="dW", inc_by=16)

            qb_free = [None, None]
            kv_free = [None, None]
            pt_free = [None, None]
            av_free = None
            st_gate = {}
            a2a_prev = {}
            a2a_toks = []

            swi = 0
            for w in range(16):
                NC = 4 * w + 4
                for sw in range(2):
                    qsel = swi % 2
                    p.wait("sync", qb_free[qsel])
                    qtok = p.run("sync", lambda E_, o=qbuf[qsel],
                                 i=self.Q_loc.ap()[sw][:, 512 * w:512 * (w + 1)]:
                                 E_.dma_start(out=o, in_=i),
                                 inc="dQ0" if qsel == 0 else "dQ1", inc_by=16)
                    if swi == 0:
                        p.wait("sync", self.tok_b1_stage)
                    exp_toks = {}
                    unit_tok = {}
                    mask_tok = {}
                    av_last = None
                    nunits = 2 * NC

                    def emit_exp(e_idx, lo_u, hi_u):
                        psel_ = e_idx % 2
                        cols0 = 512 * (lo_u % 6)
                        ncol = 512 * (hi_u - lo_u)
                        p.wait("scalar", unit_tok[hi_u - 1])
                        p.wait("scalar", pt_free[psel_])
                        tok = p.run("scalar", lambda E_, o=pt_sb[psel_][:, 0:ncol],
                                    i=st_ps[:, cols0:cols0 + ncol]:
                                    E_.activation(o, i, AF.Exp), inc="ac")
                        exp_toks[e_idx] = tok
                        return tok

                    kv_toks_k = {}
                    kv_toks_v = {}
                    for ci in range(NC):
                        for hh in range(2):
                            u = 2 * ci + hh
                            e_idx = u // 3
                            g1024 = ci // 8
                            if ci % 8 == 0 and hh == 0:
                                ksel = g1024 % 2
                                p.wait("sync", kv_free[ksel])
                                kv_toks_k[g1024] = p.run("sync", lambda E_, o=kbuf[ksel],
                                             i=self.K_loc.ap()[sw][:, 1024 * g1024:1024 * (g1024 + 1)]:
                                             E_.dma_start(out=o, in_=i),
                                             inc="dK0" if ksel == 0 else "dK1", inc_by=16)
                                vt = None
                                for hh2 in range(2):
                                    vdst = vbuf[ksel][:, 520 * hh2:520 * (hh2 + 1)].rearrange(
                                        "p (s c) -> p s c", c=65)[:, :, 0:64]
                                    vsrc = self.V_loc.ap()[2 * sw + hh2][
                                        1024 * g1024:1024 * (g1024 + 1), :].rearrange(
                                        "(s p) c -> p s c", p=128)
                                    vt = p.run("sync", lambda E_, o=vdst, i=vsrc:
                                               E_.dma_start(out=o, in_=i),
                                               inc="dV0" if ksel == 0 else "dV1", inc_by=16)
                                kv_toks_v[g1024] = vt
                            ksel = g1024 % 2
                            p.wait("tensor", kv_toks_k[g1024])
                            p.wait("tensor", qtok)
                            if swi == 0 and u == 0:
                                p.wait("tensor", vone_toks)
                            prev_e = e_idx - 2
                            if prev_e >= 0:
                                p.wait("tensor", exp_toks.get(prev_e))
                            elif swi > 0:
                                p.wait("tensor", st_gate.get(u % 6))
                            stcol = 512 * (u % 6)
                            tp = (64 * hh, 0) if hh else None
                            unit_tok[u] = p.run("tensor", lambda E_,
                                                o=st_ps[:, stcol:stcol + 512],
                                                l=kbuf[ksel][64 * hh:64 * hh + 64,
                                                             128 * (ci % 8):128 * (ci % 8 + 1)],
                                                r=qbuf[qsel][64 * hh:64 * hh + 64, :],
                                                tp=tp:
                                                E_.matmul(o, l, r, start=True, stop=True,
                                                          tile_position=tp), inc="pe")
                            if (u + 1) % 3 == 0 or u == nunits - 1:
                                lo_u = 3 * e_idx
                                hi_u = u + 1
                                etok = emit_exp(e_idx, lo_u, hi_u)
                                for uu in range(lo_u, hi_u):
                                    ci_u, hh_u = uu // 2, uu % 2
                                    g_u = ci_u // 8
                                    ksel_u = g_u % 2
                                    off = 128 * (ci_u - 4 * w) if ci_u >= 4 * w else 0
                                    pcols = 512 * (uu - lo_u)
                                    psel_ = e_idx % 2
                                    avdep = [etok]
                                    if ci_u >= 4 * w:
                                        p.wait("vector", etok)
                                        mtok = p.run("vector", lambda E_,
                                                     a=pt_sb[psel_][:, pcols + off:pcols + off + 128]:
                                                     E_.tensor_tensor(a, a, self.tri_sb[:], ALU.mult),
                                                     inc="ve")
                                        avdep.append(mtok)
                                    p.wait("tensor", avdep)
                                    p.wait("tensor", kv_toks_v[g_u])
                                    if ci_u == 0:
                                        p.wait("tensor", av_free)
                                    avtok = p.run("tensor", lambda E_,
                                                  o=av_ps[:, 512 * hh_u + off:512 * (hh_u + 1)],
                                                  l=vbuf[ksel_u][:, 520 * hh_u + 65 * (ci_u % 8):
                                                                 520 * hh_u + 65 * (ci_u % 8) + 65],
                                                  r=pt_sb[psel_][:, pcols + off:pcols + 512],
                                                  st=(ci_u == 0), sp=(ci_u == NC - 1):
                                                  E_.matmul(o, l, r, start=st, stop=sp),
                                                  inc="pe")
                                    av_last = avtok
                                    if uu == hi_u - 1:
                                        pt_free[psel_] = avtok
                                    if uu == 2 * min(8 * (g_u + 1), NC) - 1:
                                        kv_free[ksel_u] = avtok
                    qb_free[qsel] = unit_tok[nunits - 1]
                    for hh in range(2):
                        p.wait("vector", av_last)
                        rtok = p.run("vector", lambda E_, o=self.drow[32 * hh:32 * hh + 1, :],
                                     i=av_ps[64:65, 512 * hh:512 * (hh + 1)]:
                                     E_.reciprocal(o, i), inc="ve")
                    p.wait("tensor", rtok)
                    p.wait("tensor", exp_toks[ (nunits - 1) // 3 ])
                    p.wait("tensor", self.tok_consts)
                    bctok = p.run("tensor", lambda E_, o=st_ps[0:128, 0:512],
                                  l=self.sel64_sb[:], r=self.drow[:]:
                                  E_.matmul(o, l, r, start=True, stop=True), inc="pe")
                    etoks = []
                    p.wait("scalar", a2a_prev.get(sw))
                    for hh in range(2):
                        p.wait("scalar", av_last)
                        etoks.append(p.run("scalar", lambda E_,
                                           o=attnN[64 * hh:64 * (hh + 1), 512 * sw:512 * (sw + 1)],
                                           i=av_ps[0:64, 512 * hh:512 * (hh + 1)]:
                                           E_.activation(o, i, AF.Copy), inc="ac"))
                    av_free = [etoks[-1], rtok]
                    p.wait("vector", etoks)
                    p.wait("vector", bctok)
                    ntok = p.run("vector", lambda E_, a=attnN[:, 512 * sw:512 * (sw + 1)],
                                 b=st_ps[0:128, 0:512]:
                                 E_.tensor_tensor(a, a, b, ALU.mult), inc="ve")
                    for slot in range(6):
                        us = [u for u in range(nunits) if u % 6 == slot]
                        gates = []
                        if us:
                            gates.append(exp_toks[max(us) // 3])
                        if slot == 0:
                            gates.append(ntok)
                        st_gate[slot] = gates
                    c = w // 2
                    r = rank_of_chunk(c)
                    colbase = 1024 * half_of_chunk(c) + 512 * (w % 2)
                    p.wait("sync", ntok)
                    a2a_toks.append(p.run("sync", lambda E_,
                                          o=self.a2a_in.ap()[256 * r + 128 * sw:256 * r + 128 * (sw + 1),
                                                             colbase:colbase + 512],
                                          i=attnN[:, 512 * sw:512 * (sw + 1)]:
                                          E_.dma_start(out=o, in_=i),
                                          inc="dSA" if sw == 0 else "dSB", inc_by=16))
                    a2a_prev[sw] = a2a_toks[-1]
                    swi += 1

            if self.debug_taps and getattr(self, "tap_heavy", True):
                p.wait("sync", a2a_toks)
                p.run("sync", lambda E_: E_.dma_start(out=self.tap_a2a.ap(),
                                                      in_=self.a2a_in.ap()),
                      inc="dO", inc_by=16)
            p.wait("gpsimd", a2a_toks)
            self.tok_a2a = p.run("gpsimd", lambda E_: E_.collective_compute(
                "AllGather", ALU.bypass, replica_groups=RGROUPS,
                ins=[self.a2a_in.ap().opt()], outs=[self.a2a_out.ap().opt()]),
                inc="ag")

    def ln_tile(self, t_idx, halves, bias_tile, extra_res=True, ps_scale=None):
        p = self.p
        par = t_idx % 2
        s1, s2 = self.scr[par]
        cg = 8 * par
        p.wait("vector", self.ln_gate[par])
        htoks = []
        for i, (ap, tok) in enumerate(halves):
            p.wait("vector", tok)
            if bias_tile is not None:
                htoks.append(p.run("vector", lambda E_, o=s1[:, 512 * i:512 * (i + 1)], a=ap,
                                   b=bias_tile[:, 512 * i:512 * (i + 1)]:
                                   E_.scalar_tensor_tensor(o, a, 1.0, b, ALU.mult, ALU.add),
                                   inc="ve"))
            elif ps_scale is not None:
                htoks.append(p.run("vector", lambda E_, o=s1[:, 512 * i:512 * (i + 1)], a=ap,
                                   b=self.rs_tile(t_idx)[:, 512 * i:512 * (i + 1)], sc=ps_scale:
                                   E_.scalar_tensor_tensor(o, a, sc, b, ALU.mult, ALU.add),
                                   inc="ve"))
            else:
                htoks.append(p.run("vector", lambda E_, o=s1[:, 512 * i:512 * (i + 1)], a=ap,
                                   b=self.rs_tile(t_idx)[:, 512 * i:512 * (i + 1)]:
                                   E_.tensor_tensor(o, a, b, ALU.add), inc="ve"))
        p.wait("vector", htoks)
        if bias_tile is not None:
            t2 = p.run("vector", lambda E_, o=s2, a=s1, b=self.rs_tile(t_idx):
                       E_.tensor_tensor(o, a, b, ALU.add), inc="ve")
            xsrc = s2
        else:
            t2 = htoks[-1]
            xsrc = s1
        p.wait("vector", t2)
        t3 = p.run("vector", lambda E_, o=self.cols[:, cg:cg + 1], i=xsrc:
                   E_.reduce_sum(o, i, AXX, negate=True), inc="ve")
        p.wait("vector", t3)
        t4 = p.run("vector", lambda E_, o=self.cols[:, cg + 1:cg + 2], i=self.cols[:, cg:cg + 1]:
                   E_.tensor_scalar_mul(o, i, 1.0 / D), inc="ve")
        xc = s1 if xsrc is s2 else s2
        p.wait("vector", t4)
        vtok = p.run("vector", lambda E_, o=xc, i=xsrc, c=self.cols[:, cg + 1:cg + 2]:
                     E_.tensor_scalar_add(o, i, c), inc="ve")
        p.wait("scalar", self.tok_consts)
        p.wait("scalar", vtok)
        sqtok = p.run("scalar", lambda E_, o=self.junk_bf[:], i=xc,
                      a=self.cols[:, cg + 2:cg + 3]:
                      E_.activation(o, i, AF.Square, accum_out=a), inc="ac")
        p.wait("scalar", sqtok)
        lntok = p.run("scalar", lambda E_, o=self.cols[:, cg + 3:cg + 4], i=self.cols[:, cg + 2:cg + 3]:
                      E_.activation(o, i, AF.Ln, bias=EPS_LN, scale=1.0 / D), inc="ac")
        p.wait("scalar", lntok)
        rtok = p.run("scalar", lambda E_, o=self.cols[:, cg + 4:cg + 5], i=self.cols[:, cg + 3:cg + 4]:
                     E_.activation(o, i, AF.Exp, scale=-0.5), inc="ac")
        p.wait("vector", rtok)
        ftok = p.run("vector", lambda E_, o=self.rs_tile(t_idx), i=xc,
                     c=self.cols[:, cg + 4:cg + 5]:
                     E_.tensor_scalar_mul(o, i, c), inc="ve")
        self.ln_gate[par] = [sqtok, ftok]
        return ftok

    def phase_b4(self):
        p, d, nc = self.p, self.d, self.nc
        amT = self.amT_b4
        self.ln_gate = [None, None]
        self.rs_tok = [None] * NT
        tok_bm = p.run("gpsimd", lambda E_: E_.dma_start(
            out=self.bmrow, in_=d["bm"].ap().rearrange("(a d) -> a d", a=1)),
            inc="dP", inc_by=16)
        amr_sb = self.nc.alloc_sbuf_tensor("amr_sb", [1, 8], I32)
        tok_amr = p.run("gpsimd", lambda E_: E_.dma_start(out=amr_sb[:], in_=d["amrows"].ap()),
                        inc="dP", inc_by=16)
        tok_bm = ("dP", p.cnt["dP"])
        p.wait("sync", tok_amr)
        amrow_vals = []
        def loadvals(E_):
            for i in range(8):
                amrow_vals.append(E_.value_load(amr_sb[0:1, i:i + 1]))
        p.run("sync", loadvals)
        a2a3 = self.a2a_out.ap().rearrange("(b p) c -> b p c", p=128)

        with ExitStack() as ps_ctx:
            E = ps_ctx.enter_context
            cb_ps = E(nc.psum_tensor("cb_ps", [1, 1024], F32))
            bc_ps = E(nc.psum_tensor("bc_ps", [128, 512], F32))
            wm_ps = [E(nc.psum_tensor(f"wm_ps{i}", [128, 512], F32)) for i in range(2)]

            p.wait("tensor", [self.tok_wm, self.tok_colbv])
            for half in range(2):
                mtok = None
                for k in range(8):
                    mtok = p.run("tensor", lambda E_, o=cb_ps[:, 512 * half:512 * (half + 1)],
                                 l=self.colbv[:, k:k + 1],
                                 r=self.Wm_bf[:, D * k + 512 * half:D * k + 512 * (half + 1)],
                                 st=(k == 0), sp=(k == 7):
                                 E_.matmul(o, l, r, start=st, stop=sp),
                                 inc="pe" if k == 7 else None)
            p.wait("vector", mtok)
            p.wait("vector", tok_bm)
            t1 = p.run("vector", lambda E_: E_.tensor_tensor(
                self.rrow_b4, cb_ps[0:1, :], self.bmrow, ALU.add), inc="ve")
            cbtoks = []
            for half in range(2):
                p.wait("tensor", t1)
                btok = p.run("tensor", lambda E_, o=bc_ps[:],
                             l=self.ones128[:], r=self.rrow_b4[0:1, 512 * half:512 * (half + 1)]:
                             E_.matmul(o, l, r, start=True, stop=True), inc="pe")
                p.wait("vector", btok)
                ctok = p.run("vector", lambda E_, o=self.cb_tile[:, 512 * half:512 * (half + 1)],
                             i=bc_ps[:]: E_.tensor_copy(o, i), inc="ve")
                cbtoks.append(ctok)
                p.wait("tensor", ctok)
            p.wait("vector", self.tok_rs_in)
            p.wait("vector", cbtoks)
            rs_pre = []
            for t in range(NT):
                rs_pre.append(p.run("vector", lambda E_, o=self.rs_tile(t), a=self.rs_tile(t),
                                    b=self.cb_tile[:]: E_.tensor_tensor(o, a, b, ALU.add),
                                    inc="ve"))

            p.wait("sync", self.tok_a2a)
            amtoks = []
            for hp in range(8):
                amtoks.append(p.run("sync", lambda E_,
                                    o=amT[:, 2048 * hp:2048 * (hp + 1)], hp=hp:
                                    E_.dma_start(
                                        out=o.rearrange("p (a c) -> p a c", a=1),
                                        in_=a2a3[bass.ts(amrow_vals[hp], 1), :, :].rearrange(
                                            "a p c -> p a c")),
                                    inc="dM", inc_by=16))
            amtoks = [("dM", p.cnt["dM"])]
            if self.debug_taps:
                p.wait("sync", amtoks)
                p.run("sync", lambda E_: E_.dma_start(out=self.tap_amT.ap(), in_=amT),
                      inc="dO", inc_by=16)
            wm_free = [None, None]
            for v in range(4):
                for qs in range(4):
                    t_idx = 4 * v + qs
                    halves = []
                    for half in range(2):
                        psel = half
                        p.wait("tensor", amtoks)
                        p.wait("tensor", wm_free[psel])
                        mtok = None
                        for hp in range(8):
                            mtok = p.run("tensor", lambda E_, o=wm_ps[psel][:],
                                         l=amT[:, 2048 * hp + 512 * v + 128 * qs:
                                               2048 * hp + 512 * v + 128 * (qs + 1)],
                                         r=self.Wm_bf[:, D * hp + 512 * half:D * hp + 512 * (half + 1)],
                                         st=(hp == 0), sp=(hp == 7):
                                         E_.matmul(o, l, r, start=st, stop=sp),
                                         inc="pe" if hp == 7 else None)
                        halves.append((wm_ps[psel][:], mtok))
                    p.wait("vector", rs_pre[t_idx])
                    ftok = self.ln_tile(t_idx, halves, None, extra_res=False)
                    self.rs_tok[t_idx] = ftok
                    wm_free = [ftok, ftok]

    def phase_store(self):
        p = self.p
        if self.debug_taps:
            if getattr(self, "tap_heavy", True):
                p.run("sync", lambda E_: E_.dma_start(out=self.tap_a2ao.ap(),
                                                      in_=self.a2a_out.ap()),
                      inc="dO", inc_by=16)
            if self.n_blocks >= 4:
                p.run("sync", lambda E_: E_.dma_start(out=self.tap_wgT.ap(),
                                                      in_=self.wgT_d.ap()), inc="dO", inc_by=16)
                p.run("sync", lambda E_: E_.dma_start(out=self.tap_wdT.ap(),
                                                      in_=self.wdT_d.ap()), inc="dO", inc_by=16)
                p.run("sync", lambda E_, h=self.tap_h_src: E_.dma_start(
                    out=self.tap_h.ap(), in_=h), inc="dO", inc_by=16)
        p.wait("sync", [t for t in self.rs_tok if t is not None])
        src = self.RS[:].rearrange("p (t d) -> p t d", d=D)
        dst = self.y_out.ap().rearrange("(t p) d -> p t d", p=128)
        stok = p.run("sync", lambda E_: E_.dma_start(out=dst, in_=src), inc="dO", inc_by=16)
        p.wait("sync", stok)

    d_stage = 99

    def build(self):
        self.setup()
        self.phase_b1()
        self.p.barrier()
        self.phase_b3()
        self.p.barrier()
        self.phase_b4()
        self.p.barrier()
        if self.n_blocks >= 2:
            self.phase_c()
            self.p.barrier()
        if self.n_blocks >= 3:
            self.phase_d()
            self.p.barrier()
        if self.n_blocks >= 4:
            self.phase_e()
            self.p.barrier()
        self.phase_store()
        self.p.emit_block()
        return self.nc

    def ctw(self, tidx_list, xb, xT, tp_ps, tile_toks, xT_gate, tp_gate=None):
        p = self.p
        v = "vector"
        W = 128 * len(tidx_list)
        ev_last = None
        cast_gate = [None, None]
        for ti, t_idx in enumerate(tidx_list):
            bsel = ti % 2
            p.wait(v, tile_toks[t_idx])
            p.wait(v, cast_gate[bsel])
            ctok = p.run(v, lambda E_, o=xb[:, 1024 * bsel:1024 * (bsel + 1)],
                         i=self.rs_tile(t_idx): E_.tensor_copy(o, i), inc="ve")
            for gb in range(2):
                p.wait("tensor", ctok)
                p.wait("tensor", ev_last)
                p.wait("tensor", xT_gate)
                p.wait("tensor", tp_gate)
                ttok = None
                for kk4 in range(4):
                    kk = 4 * gb + kk4
                    blk = xb[:, 1024 * bsel + 128 * kk:1024 * bsel + 128 * (kk + 1)]
                    ttok = p.run("tensor", lambda E_,
                                 o=tp_ps[:, 128 * kk4:128 * (kk4 + 1)], i=blk:
                                 E_.transpose(o, i, self.id_sb[:]),
                                 inc="pe" if kk4 == 3 else None)
                p.wait(v, ttok)

                def evc(E_, gb=gb, ti=ti, W=W):
                    ins = None
                    for kk4 in range(4):
                        kk = 4 * gb + kk4
                        ins = E_.tensor_copy(xT[:, W * kk + 128 * ti:W * kk + 128 * (ti + 1)],
                                             tp_ps[:, 128 * kk4:128 * (kk4 + 1)])
                    return ins
                etok = p.run(v, evc, inc="ve")
                ev_last = etok
            cast_gate[bsel] = ev_last
        return ev_last

    def phase_c(self):
        p, d, nc = self.p, self.d, self.nc
        self.ar_reset()
        xb_c = self.bf(2048)
        xT_c = self.bf(8192)
        h_buf = self.bf(32 * 1024)
        w1buf = [self.bf(1024), self.bf(1024)]
        w2buf = [self.bf(1024), self.bf(1024)]
        self.scr = [(self.f32(1024), self.f32(1024)), (self.f32(1024), self.f32(1024))]
        s1_sb = self.f32(1024)
        c1_sb = self.f32(1024)
        tmp_sb = self.f32(1024)
        qsc = self.f32(384)
        b2row = self.f32(1024)[0:1, :]
        g = "gpsimd"
        v = "vector"

        def colload(dstap, srcname):
            srcap = d[srcname].ap().rearrange("(c p) -> p c", p=128)
            return p.run(g, lambda E_, o=dstap, i=srcap: E_.dma_start(
                out=o, in_=i, allow_slow_non_contiguous=True), inc="dP", inc_by=16)
        colload(self.thc[:], "qph")
        colload(self.wrc[:], "qwr")
        colload(self.wic[:], "qwi")
        colload(self.col1[:], "b1")
        p.run(g, lambda E_: E_.dma_start(
            out=b2row, in_=d["b2"].ap().rearrange("(a c) -> a c", a=1)),
            inc="dP", inc_by=16)
        tok_dp = ("dP", p.cnt["dP"])
        with ExitStack() as octx:
            OE = octx.enter_context
            big = [OE(nc.psum_tensor(f"Cbig{i}", [128, 1024], F32)) for i in range(2)]

            b2toks = []
            for half in range(2):
                p.wait("tensor", tok_dp)
                p.wait("tensor", b2toks)
                btok = p.run("tensor", lambda E_, o=big[0][:, 0:512],
                             l=self.ones128[:], r=b2row[0:1, 512 * half:512 * (half + 1)]:
                             E_.matmul(o, l, r, start=True, stop=True), inc="pe")
                p.wait(v, btok)
                ctok = p.run(v, lambda E_, o=self.b2_tile[:, 512 * half:512 * (half + 1)],
                             i=big[0][:, 0:512]: E_.tensor_copy(o, i), inc="ve")
                b2toks.append(ctok)
                p.wait("tensor", ctok)
            tok_b2t = b2toks[-1]

            p.wait(v, tok_dp)

            def vec(fn):
                tok = p.run(v, fn, inc="ve")
                p.wait(v, tok)
                return tok

            def rred(dst0, srccol):
                sc = qsc[:, 320:352]
                vec(lambda E_: E_.tensor_scalar(sc, srccol, 1.0 / TWO_PI, MAGIC, ALU.mult, ALU.add))
                vec(lambda E_: E_.tensor_scalar(sc, sc, MAGIC, TWO_PI, ALU.subtract, ALU.mult))
                vec(lambda E_, dst0=dst0: E_.tensor_sub(qsc[:, dst0:dst0 + 32], srccol, sc))

            def sincos(arg0, sin0, cos0):
                p.wait("scalar", ("ve", p.cnt["ve"]))
                t1 = p.run("scalar", lambda E_: E_.activation(
                    qsc[:, sin0:sin0 + 32], qsc[:, arg0:arg0 + 32], AF.Sin), inc="ac")
                vec(lambda E_: E_.tensor_scalar_mul(qsc[:, 352:384], qsc[:, arg0:arg0 + 32], 0.5))
                p.wait("scalar", ("ve", p.cnt["ve"]))
                t2 = p.run("scalar", lambda E_: E_.activation(
                    qsc[:, 192:224], qsc[:, 352:384], AF.Sin), inc="ac")
                p.wait(v, t2)
                vec(lambda E_: E_.tensor_mul(qsc[:, 224:256], qsc[:, 192:224], qsc[:, 192:224]))
                vec(lambda E_, cos0=cos0: E_.tensor_scalar(
                    qsc[:, cos0:cos0 + 32], qsc[:, 224:256], -2.0, 1.0, ALU.mult, ALU.add))

            rred(0, self.thc[:])
            vec(lambda E_: E_.tensor_scalar(qsc[:, 256:288], self.col1[:], 0.1, None, ALU.mult))
            vec(lambda E_: E_.tensor_add(qsc[:, 288:320], self.thc[:], qsc[:, 256:288]))
            rred(32, qsc[:, 288:320])
            sincos(0, 64, 96)
            sincos(32, 128, 160)
            vec(lambda E_: E_.tensor_mul(qsc[:, 192:224], qsc[:, 96:128], self.wrc[:]))
            vec(lambda E_: E_.tensor_mul(qsc[:, 224:256], qsc[:, 64:96], self.wic[:]))
            vec(lambda E_: E_.tensor_add(qsc[:, 192:224], qsc[:, 192:224], qsc[:, 224:256]))
            vec(lambda E_: E_.tensor_mul(qsc[:, 192:224], qsc[:, 192:224], self.wic[:]))
            tok_cc = vec(lambda E_: E_.tensor_scalar_mul(self.Cc[:], qsc[:, 192:224], 0.1))
            p.wait("scalar", tok_cc)
            sincol = qsc[:, 128:160]
            coscol = qsc[:, 160:192]

            w1_free = [None, None]
            w2_free = [None, None]
            hid_free = [None, None]
            hbuf_gate = None
            xT_gate = None
            tile_ln = list(self.rs_tok)

            for cw in range(2):
                tl = [8 * cw + i for i in range(8)]
                with ExitStack() as ictx:
                    tp_ps = ictx.enter_context(nc.psum_tensor(f"tpC{cw}", [128, 512], BF16))
                    xT_rdy = self.ctw(tl, xb_c, xT_c, tp_ps, tile_ln, xT_gate,
                                      tp_gate=tok_b2t if cw == 0 else None)

                relu_toks = []
                for hc in range(32):
                    bsel = hc % 2
                    p.wait(g, w1_free[bsel])
                    src = d["W1"].ap().rearrange("(k p) f -> p k f", p=128)[:, :, 128 * hc:128 * (hc + 1)]
                    dstw = w1buf[bsel].rearrange("p (k c) -> p k c", c=128)
                    wtok = p.run(g, lambda E_, o=dstw, i=src: E_.dma_start(out=o, in_=i),
                                 inc="dC0" if bsel == 0 else "dC1", inc_by=16)
                    psel = hc % 2
                    p.wait("tensor", [wtok, xT_rdy, hid_free[psel]])
                    mtok = None
                    for half in range(2):
                        for k in range(8):
                            mtok = p.run("tensor", lambda E_,
                                         o=big[psel][:, 512 * half:512 * (half + 1)],
                                         l=w1buf[bsel][:, 128 * k:128 * (k + 1)],
                                         r=xT_c[:, 1024 * k + 512 * half:1024 * k + 512 * (half + 1)],
                                         st=(k == 0), sp=(k == 7):
                                         E_.matmul(o, l, r, start=st, stop=sp),
                                         inc="pe" if (k == 7 and half == 1) else None)
                    w1_free[bsel] = mtok
                    p.wait("scalar", mtok)
                    ts1 = p.run("scalar", lambda E_, ps=big[psel]: E_.activation(
                        s1_sb, ps[:], AF.Sin, scale=0.1), inc="ac")
                    p.wait("scalar", ts1)
                    tc1 = p.run("scalar", lambda E_, ps=big[psel]: E_.activation(
                        c1_sb, ps[:], AF.Sin, scale=0.1, bias=float(np.pi / 2)), inc="ac")
                    p.wait(v, [ts1, tc1])
                    v1 = p.run(v, lambda E_, cc=hc: E_.tensor_scalar_mul(
                        tmp_sb, s1_sb, coscol[:, cc:cc + 1]), inc="ve")
                    p.wait(v, v1)
                    v2 = p.run(v, lambda E_, cc=hc: E_.scalar_tensor_tensor(
                        s1_sb, c1_sb, sincol[:, cc:cc + 1], tmp_sb, ALU.mult, ALU.add),
                        inc="ve")
                    p.wait(v, v2)
                    v3 = p.run(v, lambda E_, cc=hc, ps=big[psel]: E_.scalar_tensor_tensor(
                        c1_sb, s1_sb, self.Cc[:, cc:cc + 1], ps[:], ALU.mult, ALU.add),
                        inc="ve")
                    hid_free[psel] = v3
                    p.wait("scalar", v3)
                    p.wait("scalar", hbuf_gate)
                    rt = p.run("scalar", lambda E_, cc=hc: E_.activation(
                        h_buf[:, 1024 * cc:1024 * (cc + 1)], c1_sb, AF.Relu,
                        bias=self.col1[:, cc:cc + 1]), inc="ac")
                    relu_toks.append(rt)
                relu_all = relu_toks[-1]

                with ExitStack() as ictx:
                    ext = [ictx.enter_context(nc.psum_tensor(f"Cw{cw}_{i}", [128, 1024], F32))
                           for i in range(2)]
                    units = [big[0][:, 0:512], big[0][:, 512:1024],
                             big[1][:, 0:512], big[1][:, 512:1024],
                             ext[0][:, 0:512], ext[0][:, 512:1024],
                             ext[1][:, 0:512], ext[1][:, 512:1024]]
                    c3_gate = [hid_free[0], hid_free[1], tok_b2t]
                    for cp in range(2):
                        mt_units = [None] * 8
                        for hc in range(32):
                            bsel = hc % 2
                            p.wait(g, w2_free[bsel])
                            src = d["W2"].ap().rearrange("(h p) c -> p h c", p=128)[:, hc, :]
                            wtok = p.run(g, lambda E_, o=w2buf[bsel], i=src:
                                         E_.dma_start(out=o, in_=i),
                                         inc="dC2" if bsel == 0 else "dC3", inc_by=16)
                            p.wait("tensor", [wtok, relu_all] + (c3_gate if hc == 0 else []))
                            for u in range(8):
                                sub, half = u // 2, u % 2
                                ti = 4 * cp + sub
                                mt_units[u] = p.run("tensor", lambda E_, o=units[u],
                                                    l=h_buf[:, 1024 * hc + 128 * ti:
                                                            1024 * hc + 128 * (ti + 1)],
                                                    r=w2buf[bsel][:, 512 * half:512 * (half + 1)],
                                                    st=(hc == 0), sp=(hc == 31):
                                                    E_.matmul(o, l, r, start=st, stop=sp),
                                                    inc="pe")
                            w2_free[bsel] = mt_units[-1]
                        ln4 = []
                        for sub in range(4):
                            t_idx = 8 * cw + 4 * cp + sub
                            halves = [(units[2 * sub], mt_units[2 * sub]),
                                      (units[2 * sub + 1], mt_units[2 * sub + 1])]
                            ftok = self.ln_tile(t_idx, halves, self.b2_tile)
                            self.rs_tok[t_idx] = ftok
                            ln4.append(ftok)
                        c3_gate = [ln4[-1]]
                    hbuf_gate = c3_gate[0]
                    hid_free = [c3_gate[0], c3_gate[0]]
                    xT_gate = w1_free[1]

    def phase_d(self):
        p, d, nc = self.p, self.d, self.nc
        self.ar_reset()
        xb_d = self.bf(2048)
        xT_d = self.bf(4096)
        wqT = self.bf(2048)
        wkT = self.bf(2048)
        wvT = self.bf(2048)
        woT = self.bf(2048)
        wnat = self.bf(2048)
        qT = self.bf(4096)
        kT = self.bf(4096)
        k_tok = self.bf(4096)
        v_aug = self.bf(4160)
        y_sb = self.bf(4096)
        qtmp = self.bf(256)
        self.scr = [(self.f32(1024), self.f32(1024)), (self.f32(1024), self.f32(1024))]
        pn_sb = self.f32(256)
        pn2 = self.f32(256)
        pn3 = self.f32(256)
        gsum = self.f32(2080)
        lsum = self.f32(520)[0:64, :]
        init_f = self.f32(260)
        state_f = self.f32(130)
        state_bf = self.bf(130)
        drow64 = self.f32(128)[0:64, :]
        ncol = self.f32(16)
        g = "gpsimd"
        v = "vector"
        tile_ln = list(self.rs_tok)

        wprep_nat = self.bf(2048)
        wprep_tile = self.bf(2048)
        vone = p.run(v, lambda E_: E_.memset(
            v_aug.rearrange("p (a c) -> p a c", c=65)[:, :, 64:65], 1.0), inc="ve")
        p.wait(v, vone)
        dz = p.run(v, lambda E_: E_.memset(drow64[:, :], 0.0), inc="ve")
        p.wait(v, dz)

        with ExitStack() as octx:
            OE = octx.enter_context
            tp_ps = OE(nc.psum_tensor("tpD", [128, 512], BF16))
            pj_ps = [OE(nc.psum_tensor(f"pjD{i}", [128, 256], F32)) for i in range(2)]

            wT_toks = []
            tlast = None
            for wi_, (nm, dst) in enumerate((("Wq1", wqT), ("Wk1", wkT),
                                             ("Wv1", wvT), ("Wo1", woT))):
                p.wait(g, tlast)
                if nm != "Wo1":
                    src = d[nm].ap().rearrange("(a p) c -> p a c", p=128)
                    nat3 = wnat.rearrange("p (a c) -> p a c", c=1024)
                else:
                    src = d[nm].ap().rearrange("(a p) c -> p a c", p=128)
                    nat3 = wnat.rearrange("p (a c) -> p a c", c=256)
                wtok = p.run(g, lambda E_, o=nat3, i=src: E_.dma_start(out=o, in_=i),
                             inc="dP", inc_by=16)
                etoks = []
                for grp in range(4):
                    p.wait("tensor", wtok)
                    p.wait("tensor", etoks)
                    ttok = None
                    for j in range(4):
                        blk_i = 4 * grp + j
                        if nm != "Wo1":
                            a_, kc = blk_i // 8, blk_i % 8
                            src_blk = wnat[:, 1024 * a_ + 128 * kc:1024 * a_ + 128 * (kc + 1)]
                        else:
                            a_, c2 = blk_i // 2, blk_i % 2
                            src_blk = wnat[:, 256 * a_ + 128 * c2:256 * a_ + 128 * (c2 + 1)]
                        ttok = p.run("tensor", lambda E_, o=tp_ps[:, 128 * j:128 * (j + 1)],
                                     i=src_blk: E_.transpose(o, i, self.id_sb[:]),
                                     inc="pe" if j == 3 else None)
                    p.wait(v, ttok)

                    def evw(E_, grp=grp, nm=nm, dst=dst):
                        ins = None
                        for j in range(4):
                            blk_i = 4 * grp + j
                            if nm != "Wo1":
                                a_, kc = blk_i // 8, blk_i % 8
                                o = dst[:, 256 * kc + 128 * a_:256 * kc + 128 * (a_ + 1)]
                            else:
                                a_, c2 = blk_i // 2, blk_i % 2
                                o = dst[:, 1024 * c2 + 128 * a_:1024 * c2 + 128 * (a_ + 1)]
                            ins = E_.tensor_copy(o, tp_ps[:, 128 * j:128 * (j + 1)])
                        return ins
                    etoks.append(p.run(v, evw, inc="ve"))
                wT_toks.append(etoks[-1])
                tlast = ("ve", p.cnt["ve"])
            tok_wT = [t for t in wT_toks]

            nat_free = [None, None]
            tile_free = [None, None]
            etok = wT_toks[-1]
            wp_jobs = []
            for wi2, (nm, wdstT) in enumerate((("wg", self.wgT_d), ("wu", self.wuT_d),
                                               ("wd", self.wdT_d))):
                nchunk = 24 if nm != "wd" else 24
                for hc2 in range(24):
                    wp_jobs.append((nm, wdstT, hc2))
            for ji, (nm, wdstT, hc2) in enumerate(wp_jobs):
                bsel = ji % 2
                natb = wprep_nat[:, 1024 * bsel:1024 * (bsel + 1)]
                tileb = wprep_tile[:, 1024 * bsel:1024 * (bsel + 1)]
                p.wait(g, nat_free[bsel])
                if nm != "wd":
                    srci = d[nm].ap()[128 * hc2:128 * (hc2 + 1), :]
                    wct = p.run(g, lambda E_, o=natb, i=srci: E_.dma_start(out=o, in_=i),
                                inc="dWC", inc_by=16)
                else:
                    srci = d[nm].ap().rearrange("(k p) h -> p k h", p=128)[
                        :, :, 128 * hc2:128 * (hc2 + 1)]
                    wct = p.run(g, lambda E_, o=natb.rearrange("p (k c) -> p k c", c=128),
                                i=srci: E_.dma_start(out=o, in_=i), inc="dWC", inc_by=16)
                ttok = None
                for grp in range(2):
                    p.wait("tensor", wct)
                    p.wait("tensor", tile_free[bsel])
                    p.wait("tensor", etok)
                    for j in range(4):
                        kk = 4 * grp + j
                        ttok = p.run("tensor", lambda E_, o=tp_ps[:, 128 * j:128 * (j + 1)],
                                     i=natb[:, 128 * kk:128 * (kk + 1)]:
                                     E_.transpose(o, i, self.id_sb[:]),
                                     inc="pe" if j == 3 else None)
                    p.wait(v, ttok)

                    def evp(E_, grp=grp, tileb=tileb):
                        ins = None
                        for j in range(4):
                            kk = 4 * grp + j
                            ins = E_.tensor_copy(tileb[:, 128 * kk:128 * (kk + 1)],
                                                 tp_ps[:, 128 * j:128 * (j + 1)])
                        return ins
                    etok = p.run(v, evp, inc="ve")
                nat_free[bsel] = etok
                p.wait("sync", etok)
                if nm != "wd":
                    dsto = wdstT.ap().rearrange("(k p) h -> p k h", p=128)[
                        :, :, 128 * hc2:128 * (hc2 + 1)]
                    otok = p.run("sync", lambda E_, o=dsto,
                                 i=tileb.rearrange("p (k c) -> p k c", c=128):
                                 E_.dma_start(out=o, in_=i), inc="dE3", inc_by=16)
                else:
                    otok = p.run("sync", lambda E_, o=wdstT.ap()[128 * hc2:128 * (hc2 + 1), :],
                                 i=tileb: E_.dma_start(out=o, in_=i), inc="dE3", inc_by=16)
                tile_free[bsel] = otok
            self.tok_wprep = ("dE3", p.cnt["dE3"])

            if self.d_stage < 1:
                return
            pj_free = [None, None]
            qkv_done = None
            for dw in range(4):
                tl = [4 * dw + i for i in range(4)]
                xT_rdy = self.ctw(tl, xb_d, xT_d, tp_ps, tile_ln, qkv_done,
                                  tp_gate=("ve", p.cnt["ve"]))
                for ts4 in range(4):
                    gsub = 4 * dw + ts4
                    for pi, (wT, kind) in enumerate(((wqT, "q"), (wkT, "k"), (wvT, "v"))):
                        psel = (3 * ts4 + pi) % 2
                        p.wait("tensor", [xT_rdy, pj_free[psel]] + tok_wT)
                        mtok = None
                        for k in range(8):
                            mtok = p.run("tensor", lambda E_, o=pj_ps[psel][:],
                                         l=xT_d[:, 512 * k + 128 * ts4:512 * k + 128 * (ts4 + 1)],
                                         r=wT[:, 256 * k:256 * (k + 1)],
                                         st=(k == 0), sp=(k == 7):
                                         E_.matmul(o, l, r, start=st, stop=sp),
                                         inc="pe" if k == 7 else None)
                        if kind == "v":
                            p.wait(v, mtok)
                            va3 = v_aug.rearrange("p (a c) -> p a c", c=65)

                            def evv(E_, ps=pj_ps[psel], gsub=gsub, va3=va3):
                                ins = None
                                for h in range(4):
                                    ins = E_.tensor_copy(va3[:, 4 * gsub + h, 0:64],
                                                         ps[:, 64 * h:64 * (h + 1)])
                                return ins
                            vtok = p.run(v, evv, inc="ve")
                            pj_free[psel] = vtok
                            continue
                        p.wait("scalar", mtok)
                        t1 = p.run("scalar", lambda E_, ps=pj_ps[psel]:
                                   E_.activation(pn_sb, ps[:], AF.Relu), inc="ac")
                        p.wait("scalar", t1)
                        t2 = p.run("scalar", lambda E_, ps=pj_ps[psel]:
                                   E_.activation(pn2, ps[:], AF.Relu, scale=-1.0), inc="ac")
                        pj_free[psel] = t2
                        p.wait("scalar", t2)
                        t3 = p.run("scalar", lambda E_: E_.activation(
                            pn3, pn2, AF.Exp, scale=-1.0), inc="ac")
                        p.wait(v, t3)
                        t4 = p.run(v, lambda E_: E_.tensor_add(pn_sb, pn_sb, pn3), inc="ve")
                        p.wait("scalar", t4)
                        t5 = p.run("scalar", lambda E_: E_.activation(
                            pn3, pn_sb, AF.Square), inc="ac")
                        p.wait(v, t5)
                        t6 = p.run(v, lambda E_: E_.reduce_sum(
                            ncol[:, 0:4], pn3.rearrange("p (h c) -> p h c", c=64), AXX),
                            inc="ve")
                        p.wait("scalar", t6)
                        t7 = p.run("scalar", lambda E_: E_.activation(
                            ncol[:, 4:5].rearrange("p (a c) -> p a c", c=1)[:, 0, :]
                            if False else ncol[:, 4:8], ncol[:, 0:4], AF.Ln), inc="ac")
                        p.wait("scalar", t7)
                        t8 = p.run("scalar", lambda E_: E_.activation(
                            ncol[:, 8:12], ncol[:, 4:8], AF.Exp, scale=0.5), inc="ac")
                        p.wait(v, t8)
                        t9 = p.run(v, lambda E_: E_.tensor_scalar_add(
                            ncol[:, 12:16], ncol[:, 8:12], EPS_LA), inc="ve")
                        p.wait(v, t9)
                        t10 = p.run(v, lambda E_: E_.reciprocal(ncol[:, 0:4], ncol[:, 12:16]),
                                    inc="ve")
                        p.wait(v, t10)
                        ptoks = []
                        for h in range(4):
                            ptoks.append(p.run(v, lambda E_, h=h: E_.tensor_scalar_mul(
                                qtmp[:, 64 * h:64 * (h + 1)], pn_sb[:, 64 * h:64 * (h + 1)],
                                ncol[:, h:h + 1]), inc="ve"))
                        p.wait(v, ptoks)
                        if kind == "k":
                            ktok_cp = p.run(v, lambda E_, gsub=gsub: E_.tensor_copy(
                                k_tok[:, 256 * gsub:256 * (gsub + 1)], qtmp[:]), inc="ve")
                            p.wait(v, ktok_cp)
                        dstT = qT if kind == "q" else kT
                        p.wait("tensor", ptoks[-1] if kind == "q" else ktok_cp)
                        ttok = None
                        for b2 in range(2):
                            ttok = p.run("tensor", lambda E_, o=tp_ps[:, 128 * b2:128 * (b2 + 1)],
                                         i=qtmp[:, 128 * b2:128 * (b2 + 1)]:
                                         E_.transpose(o, i, self.id_sb[:]),
                                         inc="pe" if b2 == 1 else None)
                        p.wait(v, ttok)

                        def evt(E_, dstT=dstT, gsub=gsub):
                            ins = None
                            for b2 in range(2):
                                ins = E_.tensor_copy(
                                    dstT[:, 2048 * b2 + 128 * gsub:2048 * b2 + 128 * (gsub + 1)],
                                    tp_ps[:, 128 * b2:128 * (b2 + 1)])
                            return ins
                        evtok = p.run(v, evt, inc="ve")
                qkv_done = ("ve", p.cnt["ve"])
            p.wait("tensor", qkv_done)

        if self.d_stage < 2:
            return
        self.p.barrier()
        with ExitStack() as octx:
            kv_ps = octx.enter_context(nc.psum_tensor("kvD", [64, 130], F32))
            ev_prev = None
            for ch in range(2):
                for hp in range(2):
                    mtok = None
                    p.wait("tensor", ev_prev)
                    for hh in range(2):
                        h = 2 * hp + hh
                        for s_ in range(8):
                            gsub = 8 * ch + s_
                            mtok = p.run("tensor", lambda E_, o=kv_ps[:, 65 * hh:65 * (hh + 1)],
                                         l=k_tok[:, 256 * gsub + 64 * h:256 * gsub + 64 * (h + 1)],
                                         r=v_aug.rearrange("p (a c) -> p a c", c=65)[:, 4 * gsub + h, :],
                                         st=(s_ == 0), sp=(s_ == 7):
                                         E_.matmul(o, l, r, start=st, stop=sp),
                                         inc="pe" if s_ == 7 else None)
                    p.wait(v, mtok)
                    ev_prev = p.run(v, lambda E_, ch=ch, hp=hp: E_.tensor_copy(
                        lsum[:, 260 * ch + 130 * hp:260 * ch + 130 * (hp + 1)], kv_ps[0:64, :]),
                        inc="ve")
            p.wait("sync", ev_prev)
            lst = p.run("sync", lambda E_: E_.dma_start(out=self.ls_in.ap(), in_=lsum),
                        inc="dL", inc_by=16)
            p.wait("gpsimd", lst)
            tok_ag2 = p.run("gpsimd", lambda E_: E_.collective_compute(
                "AllGather", ALU.bypass, replica_groups=RGROUPS,
                ins=[self.ls_in.ap().opt()], outs=[self.ls_out.ap().opt()]), inc="ag")
            gts = []
            for half in range(2):
                p.wait("sync", tok_ag2)
                for r in range(4):
                    gts.append(p.run("sync", lambda E_, r=r, half=half: E_.dma_start(
                        out=gsum[64 * half:64 * (half + 1), 520 * r:520 * (r + 1)],
                        in_=self.ls_out.ap()[64 * r:64 * (r + 1), :]), inc="dG", inc_by=16))
            tok_gsum = ("dG", p.cnt["dG"])

        if self.d_stage < 3:
            return
        self.p.barrier()
        with ExitStack() as octx:
            OE = octx.enter_context
            y_ps2 = [OE(nc.psum_tensor(f"yD{i}", [65, 128], F32)) for i in range(2)]
            s_ps2 = [OE(nc.psum_tensor(f"sD{i}", [128, 128], F32)) for i in range(2)]
            b_ps = OE(nc.psum_tensor("bD", [128, 128], F32))
            d_ps2 = [OE(nc.psum_tensor(f"dD{i}", [64, 65], F32)) for i in range(2)]
            v = "vector"
            y_evac_prev = None
            st_upd_prev = None
            for ch in range(2):
                p.wait(v, tok_gsum)
                p.wait(v, st_upd_prev)
                itok = p.run(v, lambda E_: E_.memset(init_f[:, :], 0.0), inc="ve")
                p.wait(v, itok)
                for r in range(4):
                    for wch in range(2):
                        scol = self.selc[:, 8 * ch + 2 * r + wch:8 * ch + 2 * r + wch + 1]
                        itok = p.run(v, lambda E_, r=r, wch=wch, scol=scol:
                                     E_.scalar_tensor_tensor(
                                         init_f[:, :], gsum[:, 520 * r + 260 * wch:
                                                            520 * r + 260 * (wch + 1)],
                                         scol, init_f[:, :], ALU.mult, ALU.add), inc="ve")
                        p.wait(v, itok)
                sts = []
                for hp in range(2):
                    for hh in range(2):
                        sts.append(p.run(v, lambda E_, hp=hp, hh=hh: E_.tensor_copy(
                            state_f[64 * hh:64 * (hh + 1), 65 * hp:65 * (hp + 1)],
                            init_f[64 * hh:64 * (hh + 1), 130 * hp + 65 * hh:130 * hp + 65 * (hh + 1)]),
                            inc="ve"))
                p.wait(v, sts)
                sbf = p.run(v, lambda E_: E_.tensor_copy(state_bf[:], state_f[:]), inc="ve")
                p.wait(v, sbf)
                st_bf_tok = sbf
                if self.d_stage < 4:
                    continue
                for s_ in range(min(8, getattr(self, "d_sub", 8))):
                    gsub = 8 * ch + s_
                    toff = 128 * gsub
                    for hp in range(2):
                        p.wait("tensor", [st_bf_tok, y_evac_prev])
                        m1 = None
                        for hh in range(2):
                            m1 = p.run("tensor", lambda E_, hh=hh, hp=hp, toff=toff:
                                       E_.matmul(y_ps2[hh][:],
                                                 state_bf[64 * hh:64 * (hh + 1), 65 * hp:65 * (hp + 1)],
                                                 qT[64 * hh:64 * (hh + 1),
                                                    2048 * hp + toff:2048 * hp + toff + 128],
                                                 start=True, stop=False,
                                                 tile_position=(64 * hh, 0) if hh else None),
                                       inc="pe")
                        m2 = None
                        for hh in range(2):
                            m2 = p.run("tensor", lambda E_, hh=hh, hp=hp, toff=toff:
                                       E_.matmul(s_ps2[hh][:],
                                                 kT[64 * hh:64 * (hh + 1),
                                                    2048 * hp + toff:2048 * hp + toff + 128],
                                                 qT[64 * hh:64 * (hh + 1),
                                                    2048 * hp + toff:2048 * hp + toff + 128],
                                                 start=True, stop=True,
                                                 tile_position=(64 * hh, 0) if hh else None),
                                       inc="pe")
                        p.wait(v, m2)
                        mk = []
                        for hh in range(2):
                            mk.append(p.run(v, lambda E_, hh=hh: E_.tensor_tensor(
                                qtmp[:, 128 * hh:128 * (hh + 1)], s_ps2[hh][:],
                                self.tri_sb[:], ALU.mult), inc="ve"))
                        p.wait("tensor", mk)
                        m3 = None
                        for hh in range(2):
                            h = 2 * hp + hh
                            m3 = p.run("tensor", lambda E_, hh=hh, h=h, gsub=gsub:
                                       E_.matmul(y_ps2[hh][:],
                                                 v_aug.rearrange("p (a c) -> p a c", c=65)[:, 4 * gsub + h, :],
                                                 qtmp[:, 128 * hh:128 * (hh + 1)],
                                                 start=False, stop=True),
                                       inc="pe")
                        p.wait(v, m3)
                        r1 = []
                        for hh in range(2):
                            r1.append(p.run(v, lambda E_, hh=hh: E_.tensor_scalar_add(
                                drow64[32 * hh:32 * hh + 1, 0:128],
                                y_ps2[hh][64:65, :], EPS_LA), inc="ve"))
                        p.wait(v, r1)
                        r2 = []
                        for hh in range(2):
                            r2.append(p.run(v, lambda E_, hh=hh: E_.reciprocal(
                                drow64[32 * hh:32 * hh + 1, 0:128],
                                drow64[32 * hh:32 * hh + 1, 0:128]), inc="ve"))
                        p.wait("tensor", r2)
                        bc = p.run("tensor", lambda E_: E_.matmul(
                            b_ps[:], self.sel64_sb[:], drow64[:, 0:128],
                            start=True, stop=True), inc="pe")
                        p.wait("scalar", m3)
                        p.wait("scalar", y_evac_prev)
                        ev = []
                        for hh in range(2):
                            ev.append(p.run("scalar", lambda E_, hh=hh, hp=hp, toff=toff:
                                            E_.activation(
                                                y_sb[64 * hh:64 * (hh + 1),
                                                     2048 * hp + toff:2048 * hp + toff + 128],
                                                y_ps2[hh][0:64, :], AF.Copy),
                                            inc="ac"))
                        p.wait(v, ev)
                        p.wait(v, bc)
                        ymul = p.run(v, lambda E_, hp=hp, toff=toff: E_.tensor_tensor(
                            y_sb[:, 2048 * hp + toff:2048 * hp + toff + 128],
                            y_sb[:, 2048 * hp + toff:2048 * hp + toff + 128],
                            b_ps[:], ALU.mult), inc="ve")
                        y_evac_prev = ymul
                        if not (ch == 1 and s_ == 7) and self.d_stage >= 5:
                            m4 = None
                            p.wait("tensor", st_upd_prev)
                            for hh in range(2):
                                h = 2 * hp + hh
                                m4 = p.run("tensor", lambda E_, hh=hh, h=h, gsub=gsub:
                                           E_.matmul(d_ps2[hh][:],
                                                     k_tok[:, 256 * gsub + 64 * h:256 * gsub + 64 * (h + 1)],
                                                     v_aug.rearrange("p (a c) -> p a c", c=65)[:, 4 * gsub + h, :],
                                                     start=True, stop=True),
                                           inc="pe")
                            p.wait(v, m4)
                            up = []
                            for hh in range(2):
                                up.append(p.run(v, lambda E_, hh=hh, hp=hp: E_.tensor_tensor(
                                    state_f[64 * hh:64 * (hh + 1), 65 * hp:65 * (hp + 1)],
                                    d_ps2[hh][0:64, :],
                                    state_f[64 * hh:64 * (hh + 1), 65 * hp:65 * (hp + 1)],
                                    ALU.add), inc="ve"))
                            p.wait(v, up)
                            sbf2 = p.run(v, lambda E_, hp=hp: E_.tensor_copy(
                                state_bf[:, 65 * hp:65 * (hp + 1)], state_f[:, 65 * hp:65 * (hp + 1)]),
                                inc="ve")
                            st_bf_tok = sbf2
                            st_upd_prev = sbf2

        if self.d_stage < 6:
            return
        self.p.barrier()
        with ExitStack() as octx:
            wo_ps = [octx.enter_context(nc.psum_tensor(f"woD{i}", [128, 512], F32))
                     for i in range(2)]
            wo_free = [None, None]
            for tsub in range(16):
                halves = []
                for half in range(2):
                    psel = half
                    p.wait("tensor", wo_free[psel])
                    mtok = None
                    for kc in range(2):
                        mtok = p.run("tensor", lambda E_, o=wo_ps[psel][:],
                                     l=y_sb[:, 2048 * kc + 128 * tsub:2048 * kc + 128 * (tsub + 1)],
                                     r=woT[:, 1024 * kc + 512 * half:1024 * kc + 512 * (half + 1)],
                                     st=(kc == 0), sp=(kc == 1):
                                     E_.matmul(o, l, r, start=st, stop=sp),
                                     inc="pe" if kc == 1 else None)
                    halves.append((wo_ps[psel][:], mtok))
                ftok = self.ln_tile(tsub, halves, None)
                self.rs_tok[tsub] = ftok
                wo_free = [ftok, ftok]

    def phase_e(self):
        p, d, nc = self.p, self.d, self.nc
        self.ar_reset()
        xb_e = self.bf(2048)
        xT_e = self.bf(8192)
        h_buf = self.bf(24 * 1024)
        self.tap_h_src = h_buf
        wgT = [self.bf(1024), self.bf(1024)]
        wuT = [self.bf(1024), self.bf(1024)]
        wdT = [self.bf(1024), self.bf(1024)]
        sg_sb = self.bf(1024)
        self.scr = [(self.f32(1024), self.f32(1024)), (self.f32(1024), self.f32(1024))]
        g = "gpsimd"
        v = "vector"
        tile_ln = list(self.rs_tok)
        p.wait("sync", self.tok_wprep)

        with ExitStack() as octx:
            OE = octx.enter_context
            big = [OE(nc.psum_tensor(f"Ebig{i}", [128, 1024], F32)) for i in range(2)]

            gu_free = [None, None]
            wslot_free = {"g0": None, "g1": None, "u0": None, "u1": None,
                          "d0": None, "d1": None}
            hbuf_gate = None
            xT_gate = None
            for ew in range(2):
                tl = [8 * ew + i for i in range(8)]
                with ExitStack() as ictx:
                    tp_ps = ictx.enter_context(nc.psum_tensor(f"tpE{ew}", [128, 512], BF16))
                    xT_rdy = self.ctw(tl, xb_e, xT_e, tp_ps, tile_ln, xT_gate)

                if getattr(self, "e_stage", 99) < 1:
                    continue
                sm_toks = []
                for hc in range(24):
                    bsel = hc % 2
                    wts = {}
                    for nm, wbufs, slot, wtd in (("wg", wgT, "g", self.wgT_d),
                                                 ("wu", wuT, "u", self.wuT_d)):
                        p.wait("sync", wslot_free[slot + str(bsel)])
                        src = wtd.ap().rearrange("(k p) h -> p k h", p=128)[
                            :, :, 128 * hc:128 * (hc + 1)]
                        dstw = wbufs[bsel].rearrange("p (k c) -> p k c", c=128)
                        wts[nm] = p.run("sync", lambda E_, o=dstw, i=src:
                                        E_.dma_start(out=o, in_=i),
                                        inc="dE0" if slot == "g" else "dE1", inc_by=16)
                    mg = mu = None
                    for pi, (nm, wbufs, ps) in enumerate((("wg", wgT, big[0]),
                                                          ("wu", wuT, big[1]))):
                        p.wait("tensor", [wts[nm], xT_rdy, gu_free[pi]])
                        mtok = None
                        for half in range(2):
                            for k in range(8):
                                mtok = p.run("tensor", lambda E_,
                                             o=ps[:, 512 * half:512 * (half + 1)],
                                             l=wbufs[bsel][:, 128 * k:128 * (k + 1)],
                                             r=xT_e[:, 1024 * k + 512 * half:1024 * k + 512 * (half + 1)],
                                             st=(k == 0), sp=(k == 7):
                                             E_.matmul(o, l, r, start=st, stop=sp),
                                             inc="pe" if (k == 7 and half == 1) else None)
                        wslot_free[("g" if nm == "wg" else "u") + str(bsel)] = mtok
                        if pi == 0:
                            mg = mtok
                        else:
                            mu = mtok
                    p.wait("scalar", mg)
                    tsg = p.run("scalar", lambda E_: E_.activation(
                        sg_sb, big[0][:], AF.Silu, scale=1.0 / 127.0), inc="ac")
                    gu_free[0] = tsg
                    p.wait(v, [tsg, mu])
                    p.wait(v, hbuf_gate)
                    hm = p.run(v, lambda E_, cc=hc: E_.scalar_tensor_tensor(
                        h_buf[:, 1024 * cc:1024 * (cc + 1)], sg_sb, 1.0 / 127.0,
                        big[1][:], ALU.mult, ALU.mult), inc="ve")
                    gu_free[1] = hm
                    sm_toks.append(hm)
                h_all = sm_toks[-1]
                xT_gate = mu

                if getattr(self, "e_stage", 99) < 2:
                    continue
                with ExitStack() as ictx:
                    ext = [ictx.enter_context(nc.psum_tensor(f"Ew{ew}_{i}", [128, 1024], F32))
                           for i in range(2)]
                    units = [big[0][:, 0:512], big[0][:, 512:1024],
                             big[1][:, 0:512], big[1][:, 512:1024],
                             ext[0][:, 0:512], ext[0][:, 512:1024],
                             ext[1][:, 0:512], ext[1][:, 512:1024]]
                    e3_gate = [gu_free[0], gu_free[1]]
                    for cp in range(2):
                        mt_units = [None] * 8
                        for hc in range(24):
                            bsel = hc % 2
                            p.wait("sync", wslot_free["d" + str(bsel)])
                            srcT = self.wdT_d.ap()[128 * hc:128 * (hc + 1), :]
                            wtok = p.run("sync", lambda E_, o=wdT[bsel], i=srcT:
                                         E_.dma_start(out=o, in_=i),
                                         inc="dD0" if bsel == 0 else "dD1", inc_by=16)
                            p.wait("tensor", [wtok, h_all] + (e3_gate if hc == 0 else []))
                            for u in range(8):
                                sub, half = u // 2, u % 2
                                ti = 4 * cp + sub
                                mt_units[u] = p.run("tensor", lambda E_, o=units[u],
                                                    l=h_buf[:, 1024 * hc + 128 * ti:
                                                            1024 * hc + 128 * (ti + 1)],
                                                    r=wdT[bsel][:, 512 * half:512 * (half + 1)],
                                                    st=(hc == 0), sp=(hc == 23):
                                                    E_.matmul(o, l, r, start=st, stop=sp),
                                                    inc="pe")
                            wslot_free["d" + str(bsel)] = mt_units[-1]
                        ln4 = []
                        for sub in range(4):
                            t_idx = 8 * ew + 4 * cp + sub
                            halves = [(units[2 * sub], mt_units[2 * sub]),
                                      (units[2 * sub + 1], mt_units[2 * sub + 1])]
                            ftok = self.ln_tile(t_idx, halves, None, ps_scale=1.0 / 127.0)
                            self.rs_tok[t_idx] = ftok
                            ln4.append(ftok)
                        e3_gate = [ln4[-1]]
                    hbuf_gate = e3_gate[0]
                    gu_free = [e3_gate[0], e3_gate[0]]


def make_inputs_for_core(core, inp):
    g, r = divmod(core, 4)
    cA, cB = r, 7 - r
    x = np.asarray(inp["L0_attn_x"])
    xb = x[g]
    xown = np.concatenate([xb[1024 * cA:1024 * (cA + 1)], xb[1024 * cB:1024 * (cB + 1)]], axis=0)
    hsl = slice(256 * r, 256 * (r + 1))
    ident = np.eye(128).astype(ml_dtypes.bfloat16)
    trimask = np.triu(np.ones((128, 128))).astype(ml_dtypes.bfloat16)
    sel = np.zeros((128, 16), np.float32)
    for tgt_i, tgt_pos in enumerate((cA, cB)):
        for rr in range(4):
            if rr < tgt_pos:
                sel[:, 8 * tgt_i + 2 * rr] = 1.0
            if 7 - rr < tgt_pos:
                sel[:, 8 * tgt_i + 2 * rr + 1] = 1.0
    sel64 = np.zeros((64, 128), np.float32)
    sel64[0, 0:64] = 1.0
    sel64[32, 64:128] = 1.0
    constv = np.tile(np.array(CONSTVALS, np.float32), (128, 1))
    amrows = np.array([[8 * (hp // 2) + 2 * r + (hp % 2) for hp in range(8)]], np.int32)
    m = dict(
        xown=np.ascontiguousarray(xown),
        xfull=np.ascontiguousarray(xb),
        Wq_s=np.ascontiguousarray(np.asarray(inp["L0_attn_Wq"])[:, hsl]),
        bq_s=np.ascontiguousarray(np.asarray(inp["L0_attn_bq"])[hsl]),
        Wk_s=np.ascontiguousarray(np.asarray(inp["L0_attn_Wk"])[:, hsl]),
        bk_s=np.ascontiguousarray(np.asarray(inp["L0_attn_bk"])[hsl]),
        Wv_s=np.ascontiguousarray(np.asarray(inp["L0_attn_Wv"])[:, hsl]),
        Wm=np.asarray(inp["L0_attn_Wm"]), bm=np.asarray(inp["L0_attn_bm"]),
        bv=np.asarray(inp["L0_attn_bv"]),
        W1=np.asarray(inp["L0_ffn_W1"]), b1=np.asarray(inp["L0_ffn_b1"]),
        qph=np.asarray(inp["L0_ffn_quantum_phases"]),
        qwr=np.asarray(inp["L0_ffn_quantum_weights_real"]),
        qwi=np.asarray(inp["L0_ffn_quantum_weights_imag"]),
        W2=np.asarray(inp["L0_ffn_W2"]), b2=np.asarray(inp["L0_ffn_b2"]),
        Wq1=np.asarray(inp["L1_attn_Wq"]), Wk1=np.asarray(inp["L1_attn_Wk"]),
        Wv1=np.asarray(inp["L1_attn_Wv"]), Wo1=np.asarray(inp["L1_attn_Wo"]),
        wg=np.asarray(inp["L1_ffn_w_gate"]), wu=np.asarray(inp["L1_ffn_w_up"]),
        wd=np.asarray(inp["L1_ffn_w_down"]),
        ident=ident, trimask=trimask, sel=sel, sel64=sel64,
        constv=constv, amrows=amrows,
    )
    return m


def assemble_output(results):
    y = np.zeros((2, S, D), np.float32)
    for core in range(8):
        g, r = divmod(core, 4)
        cA, cB = r, 7 - r
        yo = np.asarray(results[core]["yout"], np.float32)
        y[g, 1024 * cA:1024 * (cA + 1)] = yo[:1024]
        y[g, 1024 * cB:1024 * (cB + 1)] = yo[1024:]
    return y


_CACHE = {}
_DSUB = [8]
_ESTG = [99]
_THEAVY = [True]


def build_program(n_blocks=4, debug_taps=False, d_stage=99):
    key = (n_blocks, debug_taps, d_stage)
    if key not in _CACHE:
        b = Builder(n_blocks=n_blocks, debug_taps=debug_taps)
        b.tap_heavy = _THEAVY[0]
        b.d_stage = d_stage
        b.d_sub = _DSUB[0]
        b.e_stage = _ESTG[0]
        _CACHE[key] = b.build()
    return _CACHE[key]


def kernel(**inputs):
    nc = build_program(n_blocks=4)
    in_maps = [make_inputs_for_core(c, inputs) for c in range(8)]
    res = run_bass_kernel_spmd(nc, in_maps, core_ids=list(range(8)))
    return assemble_output(res.results)
